# version 03
# baseline (speedup 1.0000x reference)
import math
import numpy as np
import ml_dtypes
from contextlib import ExitStack

import concourse.bass as bass
import concourse.mybir as mybir
from concourse.bass_utils import run_bass_kernel_spmd

F32 = mybir.dt.float32
BF16 = mybir.dt.bfloat16
I32 = mybir.dt.int32
AF = mybir.ActivationFunctionType
ALU = mybir.AluOpType
AX = mybir.AxisListType

NCORES = 8
NTOK = 2048
D = 1024
NT = NTOK // 128
H = 16
THRESH = 80.0
C7I = 1.0 / 7.5
MSTAB = 12.0

PHASES = 3
DBG_SKIP = set()


def _slopes(n_heads=16):
    start = 2 ** (-(2 ** (-(math.log2(n_heads) - 3))))
    return np.array([start * start ** i for i in range(n_heads)], dtype=np.float32)


SLOPES = _slopes(H)
NQ = []
for _h in range(H):
    _nq = 32
    for _cand in (64, 128, 256, 512):
        if SLOPES[_h] * (_cand - 1) <= 23.0:
            _nq = _cand
    NQ.append(_nq)


CVALS = [1e-5]
CIDX = {"eps": 0}
for _h in range(H):
    _s = float(SLOPES[_h])
    for _q0 in range(0, NTOK, NQ[_h]):
        _qlast = _q0 + NQ[_h] - 1
        _v = float(np.float32(_s * (2047 - _qlast))) - MSTAB
        CIDX[(_h, _q0)] = len(CVALS)
        CVALS.append(_v)
NCV = len(CVALS)


class Ctr:
    def __init__(self, sem):
        self.sem = sem
        self.n = 0


def _wait(e, tok):
    if tok is not None and tok[1] > 0:
        e.wait_ge(tok[0], tok[1])


_DRAIN_OPS = frozenset((
    "tensor_scalar", "tensor_tensor", "scalar_tensor_tensor", "tensor_copy",
    "tensor_reduce", "reciprocal", "memset", "bn_stats", "bn_aggr",
    "activation", "copy", "tensor_add", "tensor_mul", "tensor_sub",
))


class _DrainProxy:

    def __init__(self, e):
        object.__setattr__(self, "_e", e)
        object.__setattr__(self, "_w", set())

    def __getattr__(self, name):
        e = object.__getattribute__(self, "_e")
        attr = getattr(e, name)
        if name in _DRAIN_OPS:
            w = object.__getattribute__(self, "_w")

            def call(*a, **k):
                aps = [x for x in a if isinstance(x, bass.AP)]
                aps += [x for x in k.values() if isinstance(x, bass.AP)]
                if aps:
                    if any(x.name in w for x in aps[1:]):
                        e.drain()
                        w.clear()
                    w.add(aps[0].name)
                return attr(*a, **k)
            return call
        return attr


class Prog:

    def __init__(self, nc, ctx):
        self.nc = nc
        self.ops = {k: [] for k in ("sync", "scalar", "vector", "tensor", "gpsimd")}
        self.ctr = {k: Ctr(ctx.enter_context(nc.semaphore("c_" + k)))
                    for k in self.ops}
        self.cM = ctx.enter_context(nc.semaphore("c_mm"))
        self.mmn = 0

    def emit(self, eng, fn, bump=0):
        c = self.ctr[eng]
        if bump:
            c.n += bump
        self.ops[eng].append((fn, c.sem, bump))
        return (c.sem, c.n)

    def tok(self, eng):
        c = self.ctr[eng]
        return (c.sem, c.n)

    def MM(self, t, out, lhsT, rhs, start, stop):
        return t.matmul(out, lhsT, rhs, start=start, stop=stop,
                        skip_group_check=True)

    def run(self, eng_name, e):
        if eng_name in ("vector", "scalar"):
            e = _DrainProxy(e)
        for fn, semh, bump in self.ops[eng_name]:
            ins = fn(e)
            if bump:
                assert ins is not None
                ins.then_inc(semh, bump)


def build_nc():
    nc = bass.Bass("TRN2", target_bir_lowering=False,
                   dynamic_dma_scratch_size=4096,
                   detect_race_conditions=False)

    x_h = nc.dram_tensor("x", [NTOK, D], F32, kind="ExternalInput")
    wqt_h = nc.dram_tensor("wqt", [8, 128, D], F32, kind="ExternalInput")
    wkt_h = nc.dram_tensor("wkt", [8, 128, D], F32, kind="ExternalInput")
    wvt_h = nc.dram_tensor("wvt", [D, D], F32, kind="ExternalInput")
    wot_h = nc.dram_tensor("wot", [D, D], F32, kind="ExternalInput")
    wf_h = nc.dram_tensor("wf", [D, D], F32, kind="ExternalInput")
    bq_h = nc.dram_tensor("bq", [1, D], F32, kind="ExternalInput")
    bk_h = nc.dram_tensor("bk", [1, D], F32, kind="ExternalInput")
    bv_h = nc.dram_tensor("bv", [1, D], F32, kind="ExternalInput")
    bo_h = nc.dram_tensor("bo", [1, D], F32, kind="ExternalInput")
    bf_h = nc.dram_tensor("bf", [1, D], F32, kind="ExternalInput")
    alibi_h = nc.dram_tensor("alibi", [32, NTOK], BF16, kind="ExternalInput")
    maskbin_h = nc.dram_tensor("maskbin", [128, 640], BF16, kind="ExternalInput")
    maskadd_h = nc.dram_tensor("maskadd", [128, 192], BF16, kind="ExternalInput")
    ident_h = nc.dram_tensor("ident", [128, 128], BF16, kind="ExternalInput")
    cvals_h = nc.dram_tensor("cvals", [1, NCV], F32, kind="ExternalInput")
    vsbi_h = nc.dram_tensor("vsb_init", [128, NT * H * 65], BF16, kind="ExternalInput")
    out_h = nc.dram_tensor("out", [NTOK, D], F32, kind="ExternalOutput")

    ctx = ExitStack()
    sbt = lambda name, shape, dt: ctx.enter_context(nc.sbuf_tensor(name, shape, dt))
    pst = lambda name, shape, dt: ctx.enter_context(nc.psum_tensor(name, shape, dt))

    with ctx:
        x_sb = sbt("x_sb", [128, NT * D], F32)
        actT = sbt("actT", [128, 8 * NTOK], BF16)
        vsb = sbt("vsb", [128, NT * H * 65], BF16)
        qt = [sbt(f"qt{i}", [128, NTOK], BF16) for i in range(2)]
        QA = [sbt(f"QA{i}", [128, NTOK], BF16) for i in range(2)]
        KA = [sbt(f"KA{i}", [128, NTOK], BF16) for i in range(2)]
        wbuf = sbt("wbuf", [128, 8 * D], BF16)
        wstage = sbt("wstage", [128, D], F32)
        B0 = sbt("B0", [128, D], F32)
        B1 = sbt("B1", [128, D], F32)
        B5 = sbt("B5", [128, D], F32)
        Bi = sbt("Bi", [128, D], I32)
        sb1 = sbt("sb1", [128, D], BF16)
        sb2 = sbt("sb2", [128, D], BF16)
        sb3 = sbt("sb3", [128, D], BF16)
        sb4 = sbt("sb4", [128, D], BF16)
        xd = [sbt(f"xd{i}", [128, D], BF16) for i in range(2)]
        pt = [sbt(f"pt{i}", [128, 512], BF16) for i in range(2)]
        ytmp = sbt("ytmp", [128, 512], BF16)
        ytok = sbt("ytok", [128, 320], BF16)
        yn = sbt("yn", [128, 256], BF16)
        maskbin = sbt("maskbin_s", [128, 640], BF16)
        maskadd = sbt("maskadd_s", [128, 192], BF16)
        ident = sbt("ident_s", [128, 128], BF16)
        cpool = sbt("cpool", [128, NCV], F32)
        bf_row = sbt("bf_row", [1, D], BF16)
        boe_row = sbt("boe_row", [1, D], BF16)
        bq8 = sbt("bq8", [128, 8], F32)
        bk_pp = sbt("bk_pp", [128, 8], F32)
        bv_pp = sbt("bv_pp", [128, 8], BF16)
        bns = sbt("bns", [128, 12], F32)
        mv = sbt("mv", [128, 2], F32)
        stt_ = sbt("stt", [128, 8], F32)
        am = sbt("am", [128, 32], F32)
        scl = sbt("scl", [128, 32], F32)
        rcp = sbt("rcp", [128, 32], F32)

        psA = pst("psA", [128, 1024], F32)
        psB = pst("psB", [128, 1024], F32)
        psC = pst("psC", [128, 512], F32)
        psD = pst("psD", [128, 512], F32)
        psE = pst("psE", [128, 512], F32)
        psF = pst("psF", [128, 512], F32)

        P = Prog(nc, ctx)
        block = ctx.enter_context(nc.Block())

        for i in range(NT):
            P.emit("sync",
                   lambda s, i=i: s.dma_start(x_sb[:, D * i:D * (i + 1)],
                                              x_h[128 * i:128 * (i + 1), :]), 16)
        t_x = P.tok("sync")
        if "consts" not in DBG_SKIP:
            for dst, src in ((maskbin, maskbin_h),
                             (maskadd, maskadd_h), (ident, ident_h)):
                P.emit("sync", lambda s, d=dst, so=src: s.dma_start(d[:, :], so[:, :]), 16)
            P.emit("sync", lambda s: s.dma_start(
                cpool[:, :], cvals_h[:, :].broadcast_to([128, NCV])), 16)
        if "bias" not in DBG_SKIP:
            P.emit("sync", lambda s: s.dma_start(
                bq8[:, :], bq_h[0:1, :].rearrange("a (c p) -> (a p) c", p=128),
                allow_slow_non_contiguous=True), 16)
            P.emit("sync", lambda s: s.dma_start(
                bk_pp[:, :], bk_h[0:1, :].rearrange("a (c p) -> (a p) c", p=128),
                allow_slow_non_contiguous=True), 16)
            P.emit("sync", lambda s: s.dma_start(
                stt_[:, 0:8], bv_h[0:1, :].rearrange("a (c p) -> (a p) c", p=128),
                allow_slow_non_contiguous=True), 16)
            P.emit("sync", lambda s: s.dma_start(B1[0:1, 0:D], bf_h[:, :]), 16)
        t_cl = P.tok("sync")

        def f_init(v):
            _wait(v, t_cl)
            if "bias" not in DBG_SKIP:
                v.tensor_scalar(bq8[:, :], bq8[:, :], 0.125, None, op0=ALU.mult)
                v.tensor_copy(bv_pp[:, :], stt_[:, 0:8])
                v.tensor_copy(bf_row[:, :], B1[0:1, 0:D])
            return v.memset(ytmp[:, :], 0.0)
        t_init0 = P.emit("vector", f_init, 1)
        if "vsbimg" not in DBG_SKIP:
            t_vones = P.emit("sync", lambda s: s.dma_start(vsb[:, :], vsbi_h[:, :]), 16)

        tp_state = {"flip": 0, "prev": {}}

        def emit_transpose8(srcs, dest_ap, dep_tok, evac="vector"):
            prevE = tp_state["prev"].get("E")
            prevF = tp_state["prev"].get("F")

            def fmm(t, prevE=prevE, prevF=prevF, dep=dep_tok, srcs=tuple(srcs)):
                _wait(t, dep)
                _wait(t, prevE)
                _wait(t, prevF)
                for c in range(4):
                    P.MM(t, psE[:, 128 * c:128 * (c + 1)], srcs[c], ident[:, :],
                         True, True)
                    ins = P.MM(t, psF[:, 128 * c:128 * (c + 1)], srcs[4 + c],
                               ident[:, :], True, True)
                return ins
            t_mm = P.emit("tensor", fmm, 1)

            def fev(e, t_mm=t_mm, dest=dest_ap, evac=evac):
                _wait(e, t_mm)
                if isinstance(dest, tuple):
                    dE, dF = dest
                    sE = psE[:, 0:512].rearrange("p (c n) -> p c n", n=128)
                    sF = psF[:, 0:512].rearrange("p (c n) -> p c n", n=128)
                else:
                    dE, dF = dest[:, 0:512], dest[:, 512:1024]
                    sE, sF = psE[:, 0:512], psF[:, 0:512]
                if evac == "vector":
                    e.tensor_copy(dE, sE)
                    return e.tensor_copy(dF, sF)
                e.copy(dE, sE)
                return e.copy(dF, sF)
            tk = P.emit("vector" if evac == "vector" else "scalar", fev, 1)
            tp_state["prev"]["E"] = tk
            tp_state["prev"]["F"] = tk
            return tk

        def emit_quant(out_bf, rstd_col, dep_toks, out_dep):
            def f1(v, deps=tuple(dep_toks)):
                for t in deps:
                    _wait(v, t)
                x3 = B0[:, :].rearrange("p (b k) -> p b k", k=32)
                v.tensor_reduce(am[:, :], x3, axis=AX.X, op=ALU.max,
                                apply_absolute_value=True)
                v.tensor_scalar(scl[:, :], am[:, :], C7I, 1e-30, op0=ALU.mult, op1=ALU.max)
                v.tensor_scalar(rcp[:, :], am[:, :], C7I, 1e-12, op0=ALU.mult, op1=ALU.max)
                ins = v.reciprocal(rcp[:, :], rcp[:, :])
                if rstd_col is not None:
                    ins = v.tensor_scalar(scl[:, :], scl[:, :], rstd_col, None, op0=ALU.mult)
                return ins
            t_small = P.emit("vector", f1, 1)

            def f2(s, t_small=t_small):
                _wait(s, t_small)
                e1 = B1[:, :].rearrange("p (b k) -> p b k", k=32)
                e5 = B5[:, :].rearrange("p (b k) -> p b k", k=32)
                s.copy(e5, rcp[:, :].unsqueeze(2).broadcast_to([128, 32, 32]))
                return s.copy(e1, scl[:, :].unsqueeze(2).broadcast_to([128, 32, 32]))
            t_exp = P.emit("scalar", f2, 1)

            def f3(v, t_exp=t_exp):
                _wait(v, t_exp)
                return v.tensor_tensor(B0[:, :], B0[:, :], B5[:, :], op=ALU.mult)
            t_u = P.emit("vector", f3, 1)

            def f4(s, t_u=t_u):
                _wait(s, t_u)
                return s.activation(sb1[:, :], B0[:, :], AF.Square)
            t_sq = P.emit("scalar", f4, 1)

            def f5(v, t_sq=t_sq, out_dep=out_dep, out_bf=out_bf):
                _wait(v, t_sq)
                _wait(v, out_dep)
                v.tensor_scalar(sb2[:, :], sb1[:, :], 4.0, -4.0, op0=ALU.is_ge, op1=ALU.mult)
                v.tensor_scalar(sb3[:, :], sb1[:, :], 16.0, -2.0, op0=ALU.is_ge, op1=ALU.mult)
                v.scalar_tensor_tensor(sb4[:, :], sb2[:, :], 8.0, sb3[:, :],
                                       op0=ALU.add, op1=ALU.add)
                v.scalar_tensor_tensor(B0[:, :], B0[:, :], -1.0, sb4[:, :],
                                       op0=ALU.mult, op1=ALU.mult)
                v.tensor_scalar(Bi[:, :], B0[:, :], 64.5, None, op0=ALU.add)
                v.tensor_scalar(sb2[:, :], sb1[:, :], 4.0, 0.125, op0=ALU.is_ge, op1=ALU.mult)
                v.tensor_scalar(sb3[:, :], sb1[:, :], 16.0, 0.25, op0=ALU.is_ge, op1=ALU.mult)
                v.scalar_tensor_tensor(sb4[:, :], sb2[:, :], 0.125, sb3[:, :],
                                       op0=ALU.add, op1=ALU.add)
                v.tensor_copy(B5[:, :], Bi[:, :])
                v.scalar_tensor_tensor(B0[:, :], B5[:, :], -64.5, B0[:, :],
                                       op0=ALU.add, op1=ALU.is_gt)
                v.scalar_tensor_tensor(B5[:, :], B5[:, :], -1.0, B0[:, :],
                                       op0=ALU.mult, op1=ALU.add)
                v.scalar_tensor_tensor(B0[:, :], B5[:, :], 64.0, sb4[:, :],
                                       op0=ALU.add, op1=ALU.mult)
                return v.tensor_tensor(out_bf[:, :], B0[:, :], B1[:, :], op=ALU.mult)
            return P.emit("vector", f5, 1)

        def wd_stage(r):
            buf = (QA[0], QA[1], KA[0], KA[1])[r // 2]
            return buf[:, D * (r % 2):D * (r % 2 + 1)]

        def emit_wd(qa_dep):
            tq = None
            for r in range(8):
                prev_v = P.tok("vector")

                def fl(s, r=r, prev_v=prev_v):
                    _wait(s, prev_v)
                    return s.dma_start(wstage[:, :], wf_h[128 * r:128 * (r + 1), :])
                t_ld = P.emit("sync", fl, 16)

                def fc(v, t_ld=t_ld, qa_dep=qa_dep):
                    _wait(v, t_ld)
                    _wait(v, qa_dep)
                    return v.tensor_copy(B0[:, :], wstage[:, :])
                P.emit("vector", fc, 1)
                tq = emit_quant(wd_stage(r), None, [], None)
            last = None
            for kc in range(8):
                srcs = [wd_stage(r)[:, 128 * kc:128 * (kc + 1)] for r in range(8)]
                last = emit_transpose8(srcs, wbuf[:, D * kc:D * (kc + 1)], tq,
                                       evac="scalar")
            return last

        def emit_ffn(t_wd_ready):
            resid_tok = [None, None]
            xd_free = [None, None]
            pend = None

            def flush_resid(pend):
                pi, pbank, pt_mm = pend

                def fres(v, pi=pi, pbank=pbank, pt_mm=pt_mm):
                    _wait(v, pt_mm)
                    return v.scalar_tensor_tensor(x_sb[:, D * pi:D * (pi + 1)],
                                                  pbank[:, :], 0.5,
                                                  x_sb[:, D * pi:D * (pi + 1)],
                                                  op0=ALU.mult, op1=ALU.add)
                resid_tok[pi % 2] = P.emit("vector", fres, 1)

            for i in range(NT):
                xt = x_sb[:, D * i:D * (i + 1)]

                def fbn(v, xt=xt, i=i):
                    if i == 0:
                        _wait(v, t_x)
                    v.bn_stats(bns[:, 0:6], xt[:, 0:512])
                    v.bn_stats(bns[:, 6:12], xt[:, 512:1024])
                    return v.bn_aggr(mv[:, :], bns[:, :])
                t_bn = P.emit("vector", fbn, 1)

                def fsq(s, t_bn=t_bn):
                    _wait(s, t_bn)
                    return s.activation(stt_[:, 0:1], mv[:, 1:2], AF.Sqrt,
                                        bias=cpool[:, 0:1])
                t_sd = P.emit("scalar", fsq, 1)

                def frc(v, t_sd=t_sd):
                    _wait(v, t_sd)
                    v.reciprocal(stt_[:, 1:2], stt_[:, 0:1])
                    return v.tensor_scalar(stt_[:, 2:3], mv[:, 0:1], -1.0, None,
                                           op0=ALU.mult)
                t_st = P.emit("vector", frc, 1)

                def fxm(s, xt=xt, t_st=t_st):
                    _wait(s, t_st)
                    return s.activation(B0[:, :], xt, AF.Identity, bias=stt_[:, 2:3])
                t_xm = P.emit("scalar", fxm, 1)

                xb = xd[i % 2]
                tq = emit_quant(xb, stt_[:, 1:2], [t_xm], xd_free[i % 2])
                if pend is not None:
                    flush_resid(pend)
                srcs = [xb[:, 128 * c:128 * (c + 1)] for c in range(8)]
                av = actT[:, :].rearrange("p (c n) -> p c n", n=2048)
                t_tr = emit_transpose8(
                    srcs, (av[:, 0:4, 128 * i:128 * (i + 1)],
                           av[:, 4:8, 128 * i:128 * (i + 1)]), tq,
                    evac="vector")
                xd_free[i % 2] = t_tr
                bankF = psA if i % 2 == 0 else psB
                prev_res = resid_tok[i % 2]

                def fm(t, i=i, bankF=bankF, t_tr=t_tr, prev_res=prev_res,
                       t_wd_ready=t_wd_ready):
                    _wait(t, t_tr)
                    _wait(t, t_wd_ready)
                    _wait(t, prev_res)
                    for kc in range(8):
                        for nb in range(2):
                            P.MM(t, bankF[:, 512 * nb:512 * (nb + 1)],
                                 actT[:, 2048 * kc + 128 * i:2048 * kc + 128 * (i + 1)],
                                 wbuf[:, D * kc + 512 * nb:D * kc + 512 * (nb + 1)],
                                 (kc == 0), False)
                    ins = None
                    for nb in range(2):
                        ins = P.MM(t, bankF[:, 512 * nb:512 * (nb + 1)],
                                   maskbin[0:1, 512:640],
                                   bf_row[:, 512 * nb:512 * (nb + 1)], False, True)
                    return ins
                t_mm = P.emit("tensor", fm, 1)
                pend = (i, bankF, t_mm)
            flush_resid(pend)
            return P.tok("vector")

        if PHASES >= 0.5:
            t_wd1 = emit_wd(None)
        else:
            t_wd1 = None
        def f_qainit(v, dep=t_wd1):
            _wait(v, dep)
            for b in range(2):
                v.memset(QA[b][64:128, :], 0.0)
                v.memset(QA[b][64:66, :], 1.0)
            v.memset(KA[0][64:128, :], 0.0)
            return v.memset(KA[1][64:128, :], 0.0)
        t_qainit = P.emit("vector", f_qainit, 1)

        if PHASES >= 1:
            t_ffn1 = emit_ffn(t_wd1)
        else:
            def fbar(v):
                _wait(v, t_x)
                _wait(v, t_wd1)
                return v.nop()
            t_ffn1 = P.emit("vector", fbar, 1)

        def emit_attn():
            xd_free = [None, None]
            for i in range(NT):
                xt = x_sb[:, D * i:D * (i + 1)]

                def fbn(v, xt=xt):
                    v.bn_stats(bns[:, 0:6], xt[:, 0:512])
                    v.bn_stats(bns[:, 6:12], xt[:, 512:1024])
                    return v.bn_aggr(mv[:, :], bns[:, :])
                t_bn = P.emit("vector", fbn, 1)

                def fsq(s, t_bn=t_bn):
                    _wait(s, t_bn)
                    return s.activation(stt_[:, 0:1], mv[:, 1:2], AF.Sqrt,
                                        bias=cpool[:, 0:1])
                t_sd = P.emit("scalar", fsq, 1)

                def frc(v, t_sd=t_sd):
                    _wait(v, t_sd)
                    v.reciprocal(stt_[:, 1:2], stt_[:, 0:1])
                    v.tensor_tensor(stt_[:, 2:3], mv[:, 0:1], stt_[:, 1:2], op=ALU.mult)
                    return v.tensor_scalar(stt_[:, 2:3], stt_[:, 2:3], -1.0, None,
                                           op0=ALU.mult)
                t_st = P.emit("vector", frc, 1)

                lb = xd[i % 2]

                def fln(s, xt=xt, lb=lb, t_st=t_st, free=xd_free[i % 2]):
                    _wait(s, t_st)
                    _wait(s, free)
                    return s.activation(lb[:, :], xt, AF.Identity,
                                        bias=stt_[:, 2:3], scale=stt_[:, 1:2])
                t_ln = P.emit("scalar", fln, 1)
                srcs = [lb[:, 128 * c:128 * (c + 1)] for c in range(8)]
                av = actT[:, :].rearrange("p (c n) -> p c n", n=2048)
                xd_free[i % 2] = emit_transpose8(
                    srcs, (av[:, 0:4, 128 * i:128 * (i + 1)],
                           av[:, 4:8, 128 * i:128 * (i + 1)]), t_ln,
                    evac="vector")
            t_lnT = xd_free[(NT - 1) % 2]

            for kc in range(8):
                prev_v = P.tok("vector")

                def fl(s, kc=kc, prev_v=prev_v):
                    _wait(s, prev_v)
                    return s.dma_start(wstage[:, :], wvt_h[128 * kc:128 * (kc + 1), :])
                t_ld = P.emit("sync", fl, 16)

                def fc(v, kc=kc, t_ld=t_ld):
                    _wait(v, t_ld)
                    return v.tensor_copy(wbuf[:, D * kc:D * (kc + 1)], wstage[:, :])
                P.emit("vector", fc, 1)
            t_wv = P.tok("vector")
            v_ev = [None, None]
            for i in range(NT):
                for nb in range(2):
                    half = nb
                    def fmv(t, i=i, half=half, nb=nb, prev=v_ev[half], t_wv=t_wv):
                        _wait(t, t_wv)
                        _wait(t, t_lnT)
                        _wait(t, prev)
                        ins = None
                        for kc in range(8):
                            ins = t.matmul(psA[:, 512 * half:512 * (half + 1)],
                                     actT[:, 2048 * kc + 128 * i:2048 * kc + 128 * (i + 1)],
                                     wbuf[:, D * kc + 512 * nb:D * kc + 512 * (nb + 1)],
                                     start=(kc == 0), stop=(kc == 7))
                        return ins
                    t_mmv = P.emit("tensor", fmv, 1)

                    def fev(v, i=i, half=half, nb=nb, t_mmv=t_mmv):
                        _wait(v, t_mmv)
                        va = vsb[:, :].rearrange("p (t h c) -> p t h c", t=NT, c=65)
                        src_ = psA[:, 512 * half:512 * (half + 1)].rearrange(
                            "p (h c) -> p h c", c=64)
                        return v.tensor_copy(va[:, i, 8 * nb:8 * (nb + 1), 0:64], src_)
                    v_ev[half] = P.emit("vector", fev, 1)
            t_vp = P.tok("vector")

            blocks = {}
            for h in range(H):
                s_h = float(SLOPES[h])
                nq = NQ[h]
                bl = []
                for q0 in range(0, NTOK, nq):
                    qlast = q0 + nq - 1
                    t_hi = qlast // 128
                    t_lo = max(0, int(math.ceil((q0 - THRESH / s_h - 127.0) / 128.0)))
                    bl.append((q0, t_lo, t_hi))
                blocks[h] = bl

            qk_prev = [v_ev[0], v_ev[1]]
            t_rehoused = [None, None]
            t_attn_done = [None, None]
            wqk_cast = [None, None]
            st_prev = [None, None]
            pt_prev = [None, None]
            y_ev_bank = [None, None]
            tp_y = tp_state

            for m in range(8):
                par = m % 2
                prev_cast = wqk_cast[par]

                def flq(s, m=m, prev_cast=prev_cast):
                    _wait(s, prev_cast)
                    if m == 0:
                        _wait(s, t_vp)
                    return s.dma_start(wstage[:, :], wqt_h[m, :, :])
                t_lq = P.emit("sync", flq, 16)

                def fcq(v, par=par, t_lq=t_lq):
                    _wait(v, t_lq)
                    return v.tensor_copy(wbuf[:, 1024 * par:1024 * (par + 1)], wstage[:, :])
                t_cq = P.emit("vector", fcq, 1)

                def flk(s, m=m, t_cq=t_cq):
                    _wait(s, t_cq)
                    return s.dma_start(wstage[:, :], wkt_h[m, :, :])
                t_lk = P.emit("sync", flk, 16)

                def fck(v, par=par, t_lk=t_lk):
                    _wait(v, t_lk)
                    return v.tensor_copy(wbuf[:, 2048 + 1024 * par:2048 + 1024 * (par + 1)],
                                         wstage[:, :])
                wqk_cast[par] = P.emit("vector", fck, 1)
                t_ck = wqk_cast[par]

                reh_dep = t_rehoused[par]
                t_rh_parts = []
                for which, wofs, bias_ap, scale in (
                        ("q", 1024 * par, bq8, 0.125),
                        ("k", 2048 + 1024 * par, bk_pp, 1.0)):
                    if which == "k":
                        reh_dep = t_rh_parts[0]
                    for g4 in range(4):
                        half = g4 % 2

                        def fmq(t, wofs=wofs, g4=g4, half=half, t_ck=t_ck,
                                prev=qk_prev[half], reh_dep=reh_dep):
                            _wait(t, t_ck)
                            _wait(t, t_lnT)
                            _wait(t, prev)
                            _wait(t, reh_dep)
                            ins = None
                            for kc in range(8):
                                ins = t.matmul(psA[:, 512 * half:512 * (half + 1)],
                                         wbuf[:, wofs + 128 * kc:wofs + 128 * (kc + 1)],
                                         actT[:, 2048 * kc + 512 * g4:2048 * kc + 512 * (g4 + 1)],
                                         start=(kc == 0), stop=(kc == 7),
                                         skip_group_check=True)
                            return ins
                        t_mq = P.emit("tensor", fmq, 1)

                        def fevq(s, bias_ap=bias_ap, scale=scale, par=par,
                                 g4=g4, half=half, m=m, t_mq=t_mq, reh_dep=reh_dep):
                            _wait(s, t_mq)
                            _wait(s, reh_dep)
                            return s.activation(qt[par][:, 512 * g4:512 * (g4 + 1)],
                                                psA[:, 512 * half:512 * (half + 1)],
                                                AF.Identity, bias=bias_ap[:, m:m + 1],
                                                scale=scale)
                        qk_prev[half] = P.emit("scalar", fevq, 1)
                    t_qk = P.tok("scalar")

                    def frh_v(v, which=which, par=par, t_qk=t_qk,
                              prev_att0=t_attn_done[0], prev_att1=t_attn_done[1]):
                        _wait(v, t_qk)
                        _wait(v, prev_att0)
                        _wait(v, prev_att1)
                        if which == "q":
                            _wait(v, t_qainit)
                        ins = None
                        for hh in range(2):
                            buf = QA[hh] if which == "q" else KA[hh]
                            ins = v.tensor_copy(buf[0:64, :],
                                                qt[par][64 * hh:64 * (hh + 1), :])
                        return ins
                    t_rhv = P.emit("vector", frh_v, 1)
                    if which == "k":
                        def frh_g(g, m=m, t_rhv=t_rhv):
                            _wait(g, t_rhv)
                            for hh in range(2):
                                h = 2 * m + hh
                                g.dma_start(KA[hh][64:65, :],
                                            alibi_h[h:h + 1, :]).then_inc(P.ctr["sync"].sem, 16)
                                ins = g.dma_start(KA[hh][65:66, :],
                                                  alibi_h[16 + h:17 + h, :])
                                if hh == 0:
                                    ins.then_inc(P.ctr["sync"].sem, 16)
                            return ins
                        P.ctr["sync"].n += 16 * 3
                        t_rh_parts.append(P.emit("sync", frh_g, 16))
                    else:
                        t_rh_parts.append(t_rhv)
                t_rehoused[par] = t_rh_parts[1]
                t_rh = t_rehoused[par]

                for hh in range(2):
                    h = 2 * m + hh
                    s_h = float(SLOPES[h])
                    nq = NQ[h]
                    gsize = max(1, 512 // nq)
                    for qg in range(NTOK // 512):
                        yb = qg % 2
                        ybank = psC if yb == 0 else psD
                        q0s = [b for b in blocks[h] if 512 * qg <= b[0] < 512 * (qg + 1)]
                        first_of_bank = True
                        prev_y_ev = y_ev_bank[yb]
                        for (q0, t_lo, t_hi) in q0s:
                            qlast = q0 + nq - 1
                            cb_ap = cpool[:, CIDX[(h, q0)]:CIDX[(h, q0)] + 1]
                            ts_list = list(range(t_lo, t_hi + 1))
                            total_pv = len(ts_list)
                            pv_done = 0
                            for gi in range(0, len(ts_list), gsize):
                                grp = ts_list[gi:gi + gsize]
                                sh = st_flip[0]
                                st_flip[0] ^= 1

                                def fst(t, grp=grp, sh=sh, q0=q0, nq=nq, hh=hh,
                                        prev_exp=st_prev[sh], t_rh=t_rh):
                                    _wait(t, t_rh)
                                    _wait(t, prev_exp)
                                    ins = None
                                    for j, tt in enumerate(grp):
                                        ins = t.matmul(
                                            psB[:, 512 * sh + nq * j:512 * sh + nq * (j + 1)],
                                            KA[hh][:, 128 * tt:128 * (tt + 1)],
                                            QA[hh][:, q0:q0 + nq],
                                            start=(j == 0), stop=(j == len(grp) - 1))
                                    return ins
                                t_stm = P.emit("tensor", fst, 1)

                                t_pm = t_stm
                                diag = [(j, tt) for j, tt in enumerate(grp)
                                        if 128 * tt + 127 > q0]
                                if h == 0 and diag:
                                    def fpm(v, diag=diag, sh=sh, q0=q0, nq=nq,
                                            t_stm=t_stm):
                                        _wait(v, t_stm)
                                        for j, tt in diag:
                                            w = min(q0 + nq - 1, 128 * tt + 126) - q0 + 1
                                            jo = q0 - 128 * tt + 32
                                            ins = v.tensor_tensor(
                                                psB[:, 512 * sh + nq * j:512 * sh + nq * j + w],
                                                psB[:, 512 * sh + nq * j:512 * sh + nq * j + w],
                                                maskadd[:, jo:jo + w], op=ALU.add)
                                        return ins
                                    t_pm = P.emit("vector", fpm, 1)

                                pb = sh
                                gw = nq * len(grp)

                                def fex(s, sh=sh, pb=pb, gw=gw, cb_ap=cb_ap,
                                        t_pm=t_pm, prev_pv=pt_prev[pb]):
                                    _wait(s, t_pm)
                                    _wait(s, prev_pv)
                                    return s.activation(pt[pb][:, 0:gw],
                                                        psB[:, 512 * sh:512 * sh + gw],
                                                        AF.Exp, bias=cb_ap)
                                t_exp = P.emit("scalar", fex, 1)
                                st_prev[sh] = t_exp

                                t_msk = t_exp
                                if h != 0 and diag:
                                    def fpo(v, diag=diag, pb=pb, q0=q0, nq=nq,
                                            t_exp=t_exp):
                                        _wait(v, t_exp)
                                        for j, tt in diag:
                                            w = min(q0 + nq - 1, 128 * tt + 126) - q0 + 1
                                            jo = q0 - 128 * tt + 512
                                            ins = v.tensor_tensor(
                                                pt[pb][:, nq * j:nq * j + w],
                                                pt[pb][:, nq * j:nq * j + w],
                                                maskbin[:, jo:jo + w], op=ALU.mult)
                                        return ins
                                    t_msk = P.emit("vector", fpo, 1)

                                def fpv(t, grp=grp, pb=pb, q0=q0, nq=nq, qg=qg,
                                        ybank=ybank, t_msk=t_msk, h=h,
                                        first=first_of_bank, pv0=pv_done,
                                        total=total_pv, prev_y_ev=prev_y_ev):
                                    _wait(t, t_msk)
                                    if first:
                                        _wait(t, prev_y_ev)
                                    ins = None
                                    for j, tt in enumerate(grp):
                                        ins = t.matmul(
                                            ybank[0:65, q0 - 512 * qg:q0 - 512 * qg + nq],
                                            vsb[:, (H * 65) * tt + 65 * h:(H * 65) * tt + 65 * (h + 1)],
                                            pt[pb][:, nq * j:nq * (j + 1)],
                                            start=(pv0 + j == 0),
                                            stop=(pv0 + j == total - 1),
                                            skip_group_check=True)
                                    return ins
                                pt_prev[pb] = P.emit("tensor", fpv, 1)
                                first_of_bank = False
                                pv_done += len(grp)
                        t_pv_last = P.tok("tensor")

                        def fye(s, ybank=ybank, t_pv_last=t_pv_last):
                            _wait(s, t_pv_last)
                            return s.activation(ytmp[0:65, :], ybank[0:65, :], AF.Identity)
                        t_ye = P.emit("scalar", fye, 1)

                        prevE = tp_state["prev"].get("E")
                        prevF = tp_state["prev"].get("F")

                        def fyt(t, t_ye=t_ye, prevE=prevE, prevF=prevF):
                            _wait(t, t_ye)
                            _wait(t, prevE)
                            _wait(t, prevF)
                            P.MM(t, psE[:, 0:80], ytmp[0:80, 0:128],
                                 ident[0:80, 0:80], True, True)
                            P.MM(t, psF[:, 0:80], ytmp[0:80, 256:384],
                                 ident[0:80, 0:80], True, True)
                            P.MM(t, psE[:, 80:160], ytmp[0:80, 128:256],
                                 ident[0:80, 0:80], True, True)
                            return P.MM(t, psF[:, 80:160], ytmp[0:80, 384:512],
                                        ident[0:80, 0:80], True, True)
                        t_yt = P.emit("tensor", fyt, 1)

                        def fyv(v, t_yt=t_yt):
                            _wait(v, t_yt)
                            v.tensor_copy(ytok[:, 0:160], psE[:, 0:160])
                            v.tensor_copy(ytok[:, 160:320], psF[:, 0:160])
                            sc_ap = ytok[:, :].rearrange("p (c k) -> p c k", k=80)[:, :, 64:65].squeeze(2)
                            v.tensor_scalar(stt_[:, 4:8], sc_ap, 8192.0, None, op0=ALU.mult)
                            v.reciprocal(stt_[:, 4:8], stt_[:, 4:8])
                            v.tensor_scalar(stt_[:, 4:8], stt_[:, 4:8], 8192.0, None, op0=ALU.mult)
                            ins = None
                            for c in range(4):
                                cc = c
                                ins = v.tensor_scalar(yn[:, 64 * cc:64 * (cc + 1)],
                                                      ytok[:, 80 * c:80 * c + 64],
                                                      stt_[:, 4 + c:5 + c], None,
                                                      op0=ALU.mult)
                            return ins
                        t_yn = P.emit("vector", fyv, 1)
                        tp_state["prev"]["E"] = t_yn
                        tp_state["prev"]["F"] = t_yn

                        def fyb(t, t_yn=t_yn):
                            _wait(t, t_yn)
                            P.MM(t, psE[0:64, 0:128], yn[:, 0:64], ident[:, :],
                                 True, True)
                            P.MM(t, psF[0:64, 0:128], yn[:, 128:192], ident[:, :],
                                 True, True)
                            P.MM(t, psE[0:64, 128:256], yn[:, 64:128], ident[:, :],
                                 True, True)
                            return P.MM(t, psF[0:64, 128:256], yn[:, 192:256],
                                        ident[:, :], True, True)
                        t_yb = P.emit("tensor", fyb, 1)

                        def fyw2(v, h=h, qg=qg, t_yb=t_yb):
                            _wait(v, t_yb)
                            full = yt_ap_full(h // 2, 512 * qg, 512)
                            r0 = 64 * (h % 2)
                            v.tensor_copy(full[r0:r0 + 64, 0:256], psE[0:64, 0:256])
                            return v.tensor_copy(full[r0:r0 + 64, 256:512],
                                                 psF[0:64, 0:256])
                        t_fyw = P.emit("vector", fyw2, 1)
                        tp_state["prev"]["E"] = t_fyw
                        tp_state["prev"]["F"] = t_fyw
                        y_ev_bank[yb] = t_fyw
                    t_attn_done[hh] = P.tok("tensor")
            t_attn_v = P.tok("vector")
            t_attn_s = P.tok("scalar")

            for kc in range(8):
                prev_v = P.tok("vector")

                def fl(s, kc=kc, prev_v=prev_v):
                    _wait(s, prev_v)
                    return s.dma_start(wstage[:, :], wot_h[128 * kc:128 * (kc + 1), :])
                t_ld = P.emit("sync", fl, 16)

                def fc(v, kc=kc, t_ld=t_ld):
                    _wait(v, t_ld)
                    return v.tensor_copy(wbuf[:, D * kc:D * (kc + 1)], wstage[:, :])
                P.emit("vector", fc, 1)
            t_wo = P.tok("vector")

            def fbvw(t, t_wo=t_wo, t_attn_s=t_attn_s):
                _wait(t, t_wo)
                _wait(t, t_attn_s)
                for nb in range(2):
                    for kc in range(8):
                        ins = t.matmul(psB[0:1, 512 * nb:512 * (nb + 1)],
                                 bv_pp[:, kc:kc + 1],
                                 wbuf[:, D * kc + 512 * nb:D * kc + 512 * (nb + 1)],
                                 start=(kc == 0), stop=(kc == 7))
                return ins
            t_bvw = P.emit("tensor", fbvw, 1)

            def flbo(s, t_wo=t_wo):
                _wait(s, t_wo)
                return s.dma_start(wstage[0:1, 0:D], bo_h[:, :])
            t_bo2 = P.emit("sync", flbo, 16)

            def fboe(v, t_bvw=t_bvw, t_bo2=t_bo2):
                _wait(v, t_bvw)
                _wait(v, t_bo2)
                return v.tensor_tensor(boe_row[:, :], wstage[0:1, 0:D],
                                       psB[0:1, 0:D], op=ALU.add)
            t_boe = P.emit("vector", fboe, 1)

            mm_prev = [None, None]
            for i in range(NT):
                xt = x_sb[:, D * i:D * (i + 1)]
                bankF = psA if i % 2 == 0 else psB

                def fm(t, i=i, bankF=bankF, prev=mm_prev[i % 2], t_boe=t_boe,
                       t_attn_v=t_attn_v):
                    _wait(t, t_boe)
                    _wait(t, t_attn_v)
                    _wait(t, prev)
                    ins = None
                    for nb in range(2):
                        for kc in range(8):
                            t.matmul(bankF[:, 512 * nb:512 * (nb + 1)],
                                     yt_ap_full(kc, 128 * i, 128),
                                     wbuf[:, D * kc + 512 * nb:D * kc + 512 * (nb + 1)],
                                     start=(kc == 0), stop=False)
                        ins = t.matmul(bankF[:, 512 * nb:512 * (nb + 1)],
                                 maskbin[0:1, 512:640], boe_row[:, 512 * nb:512 * (nb + 1)],
                                 start=False, stop=True)
                    return ins
                t_mm = P.emit("tensor", fm, 1)

                def fres(v, xt=xt, bankF=bankF, t_mm=t_mm):
                    _wait(v, t_mm)
                    return v.scalar_tensor_tensor(xt, bankF[:, :], 1.0, xt,
                                                  op0=ALU.mult, op1=ALU.add)
                mm_prev[i % 2] = P.emit("vector", fres, 1)
            return P.tok("vector")

        def yt_ap_full(c, col0, width):
            homes = {
                0: bass.AP(B0, 0, [[D, 128], [1, D]]).bitcast(BF16),
                1: bass.AP(B1, 0, [[D, 128], [1, D]]).bitcast(BF16),
                2: bass.AP(B5, 0, [[D, 128], [1, D]]).bitcast(BF16),
                3: bass.AP(Bi, 0, [[D, 128], [1, D]]).bitcast(BF16),
                6: qt[0][:, :],
                7: qt[1][:, :],
            }
            if c in homes:
                return homes[c][:, col0:col0 + width]
            if c == 4:
                tns = sb1 if col0 < 1024 else sb2
            else:
                tns = sb3 if col0 < 1024 else sb4
            o = col0 % 1024
            return tns[:, o:o + width]

        st_flip = [0]

        if PHASES >= 2:
            t_attn = emit_attn()
        else:
            t_attn = t_ffn1

        if PHASES >= 3:
            t_wd2 = emit_wd(t_attn)
            t_ffn2 = emit_ffn(t_wd2)
            t_final = t_ffn2
        else:
            t_final = t_attn

        for i in range(NT):
            def fst_(s, i=i, t_final=t_final):
                if i == 0:
                    _wait(s, t_final)
                return s.dma_start(out_h[128 * i:128 * (i + 1), :],
                                   x_sb[:, D * i:D * (i + 1)])
            P.emit("sync", fst_, 16)
        t_done = P.tok("sync")

        def ffin(s):
            _wait(s, t_done)
            return None
        P.emit("sync", ffin, 0)

        @block.sync
        def _(sync):
            P.run("sync", sync)

        @block.scalar
        def _(scalar):
            P.run("scalar", scalar)

        @block.vector
        def _(vector):
            P.run("vector", vector)

        @block.tensor
        def _(tensor):
            P.run("tensor", tensor)

        @block.gpsimd
        def _(gpsimd):
            P.run("gpsimd", gpsimd)

    return nc


def _vsb_init():
    v = np.zeros((128, NT * H * 65), dtype=ml_dtypes.bfloat16)
    v3 = v.reshape(128, NT * H, 65)
    v3[:, :, 64] = 1.0
    return v


def _host_consts():
    k = np.arange(NTOK, dtype=np.float64)
    alibi = np.zeros((32, NTOK), dtype=ml_dtypes.bfloat16)
    for h in range(H):
        v = (np.float32(SLOPES[h]) * (k - 2047.0).astype(np.float32)).astype(np.float32)
        A = (np.round(v / 8.0) * 8.0).astype(np.float32)
        B = (v - A).astype(np.float32)
        alibi[h] = A.astype(ml_dtypes.bfloat16)
        alibi[16 + h] = B.astype(ml_dtypes.bfloat16)
    kk = np.arange(128)[:, None]
    jj = np.arange(640)[None, :]
    maskbin = (kk <= (jj - 512)).astype(ml_dtypes.bfloat16)
    jj3 = np.arange(480, 672)[None, :]
    maskadd = np.where(kk <= (jj3 - 512), np.float32(0.0),
                       np.float32(-1e30)).astype(ml_dtypes.bfloat16)
    ident = np.eye(128, dtype=ml_dtypes.bfloat16)
    return alibi, maskbin, maskadd, ident


def kernel(L0_attn_x, L0_attn_Wq, L0_attn_bq, L0_attn_Wk, L0_attn_bk,
           L0_attn_Wv, L0_attn_bv, L0_attn_Wo, L0_attn_bo,
           L0_ffn_weight, L0_ffn_bias):
    x = np.ascontiguousarray(np.asarray(L0_attn_x, dtype=np.float32))
    wq = np.asarray(L0_attn_Wq, np.float32)
    wk = np.asarray(L0_attn_Wk, np.float32)
    wv = np.asarray(L0_attn_Wv, np.float32)
    wo = np.asarray(L0_attn_Wo, np.float32)
    wf = np.ascontiguousarray(np.asarray(L0_ffn_weight, np.float32))
    wqt = np.ascontiguousarray(
        wq.T.reshape(8, 128, 8, 128).transpose(2, 1, 0, 3).reshape(8, 128, D))
    wkt = np.ascontiguousarray(
        wk.T.reshape(8, 128, 8, 128).transpose(2, 1, 0, 3).reshape(8, 128, D))
    wvt = np.ascontiguousarray(wv.T)
    wot = np.ascontiguousarray(wo.T)
    alibi, maskbin, maskadd, ident = _host_consts()

    common = {
        "wqt": wqt, "wkt": wkt, "wvt": wvt, "wot": wot, "wf": wf,
        "bq": np.asarray(L0_attn_bq, np.float32).reshape(1, D),
        "bk": np.asarray(L0_attn_bk, np.float32).reshape(1, D),
        "bv": np.asarray(L0_attn_bv, np.float32).reshape(1, D),
        "bo": np.asarray(L0_attn_bo, np.float32).reshape(1, D),
        "bf": np.asarray(L0_ffn_bias, np.float32).reshape(1, D),
        "alibi": alibi, "maskbin": maskbin, "maskadd": maskadd, "ident": ident,
        "cvals": np.asarray(CVALS, dtype=np.float32).reshape(1, NCV),
        "vsb_init": _vsb_init(),
    }
    in_maps = [dict(common, x=np.ascontiguousarray(x[c])) for c in range(NCORES)]

    nc = build_nc()
    res = run_bass_kernel_spmd(nc, in_maps, core_ids=list(range(NCORES)))
    return np.stack([np.asarray(r["out"], dtype=np.float32)
                     for r in res.results], axis=0)


# version 04 (final)
# speedup vs baseline: 1.2122x; 1.2122x over previous
import math
import numpy as np
import ml_dtypes
from contextlib import ExitStack

import concourse.bass as bass
import concourse.mybir as mybir
from concourse.bass_utils import run_bass_kernel_spmd

F32 = mybir.dt.float32
BF16 = mybir.dt.bfloat16
I32 = mybir.dt.int32
AF = mybir.ActivationFunctionType
ALU = mybir.AluOpType
AX = mybir.AxisListType

NCORES = 8
NTOK = 2048
D = 1024
NT = NTOK // 128
H = 16
THRESH = 80.0
C7I = 1.0 / 7.5
MSTAB = 12.0

PHASES = 3
DBG_SKIP = set()


def _slopes(n_heads=16):
    start = 2 ** (-(2 ** (-(math.log2(n_heads) - 3))))
    return np.array([start * start ** i for i in range(n_heads)], dtype=np.float32)


SLOPES = _slopes(H)
NQ = []
for _h in range(H):
    _nq = 32
    for _cand in (64, 128, 256, 512):
        if SLOPES[_h] * (_cand - 1) <= 23.0:
            _nq = _cand
    NQ.append(_nq)


CVALS = [1e-5]
CIDX = {"eps": 0}
for _h in range(H):
    _s = float(SLOPES[_h])
    for _q0 in range(0, NTOK, NQ[_h]):
        _qlast = _q0 + NQ[_h] - 1
        _v = float(np.float32(_s * (2047 - _qlast))) - MSTAB
        CIDX[(_h, _q0)] = len(CVALS)
        CVALS.append(_v)
NCV = len(CVALS)


class Ctr:
    def __init__(self, sem):
        self.sem = sem
        self.n = 0


def _wait(e, tok):
    if tok is not None and tok[1] > 0:
        e.wait_ge(tok[0], tok[1])


_DRAIN_OPS = frozenset((
    "tensor_scalar", "tensor_tensor", "scalar_tensor_tensor", "tensor_copy",
    "tensor_reduce", "reciprocal", "memset", "bn_stats", "bn_aggr",
    "activation", "copy", "tensor_add", "tensor_mul", "tensor_sub",
))


class _DrainProxy:

    def __init__(self, e):
        object.__setattr__(self, "_e", e)
        object.__setattr__(self, "_w", set())

    def __getattr__(self, name):
        e = object.__getattribute__(self, "_e")
        attr = getattr(e, name)
        if name in _DRAIN_OPS:
            w = object.__getattribute__(self, "_w")

            def call(*a, **k):
                aps = [x for x in a if isinstance(x, bass.AP)]
                aps += [x for x in k.values() if isinstance(x, bass.AP)]
                if aps:
                    if any(x.name in w for x in aps[1:]):
                        e.drain()
                        w.clear()
                    w.add(aps[0].name)
                return attr(*a, **k)
            return call
        return attr


class Prog:

    def __init__(self, nc, ctx):
        self.nc = nc
        self.ops = {k: [] for k in ("sync", "scalar", "vector", "tensor", "gpsimd")}
        self.ctr = {k: Ctr(ctx.enter_context(nc.semaphore("c_" + k)))
                    for k in self.ops}
        self.cM = ctx.enter_context(nc.semaphore("c_mm"))
        self.mmn = 0

    def emit(self, eng, fn, bump=0):
        c = self.ctr[eng]
        if bump:
            c.n += bump
        self.ops[eng].append((fn, c.sem, bump))
        return (c.sem, c.n)

    def tok(self, eng):
        c = self.ctr[eng]
        return (c.sem, c.n)

    def MM(self, t, out, lhsT, rhs, start, stop):
        return t.matmul(out, lhsT, rhs, start=start, stop=stop,
                        skip_group_check=True)

    def run(self, eng_name, e):
        if eng_name in ("vector", "scalar"):
            e = _DrainProxy(e)
        for fn, semh, bump in self.ops[eng_name]:
            ins = fn(e)
            if bump:
                assert ins is not None
                ins.then_inc(semh, bump)


def build_nc():
    nc = bass.Bass("TRN2", target_bir_lowering=False,
                   dynamic_dma_scratch_size=4096,
                   detect_race_conditions=False)

    x_h = nc.dram_tensor("x", [NTOK, D], F32, kind="ExternalInput")
    wqt_h = nc.dram_tensor("wqt", [8, 128, D], F32, kind="ExternalInput")
    wkt_h = nc.dram_tensor("wkt", [8, 128, D], F32, kind="ExternalInput")
    wvt_h = nc.dram_tensor("wvt", [D, D], F32, kind="ExternalInput")
    wot_h = nc.dram_tensor("wot", [D, D], F32, kind="ExternalInput")
    wf_h = nc.dram_tensor("wf", [D, D], F32, kind="ExternalInput")
    bq_h = nc.dram_tensor("bq", [1, D], F32, kind="ExternalInput")
    bk_h = nc.dram_tensor("bk", [1, D], F32, kind="ExternalInput")
    bv_h = nc.dram_tensor("bv", [1, D], F32, kind="ExternalInput")
    bo_h = nc.dram_tensor("bo", [1, D], F32, kind="ExternalInput")
    bf_h = nc.dram_tensor("bf", [1, D], F32, kind="ExternalInput")
    alibi_h = nc.dram_tensor("alibi", [32, NTOK], BF16, kind="ExternalInput")
    maskbin_h = nc.dram_tensor("maskbin", [128, 640], BF16, kind="ExternalInput")
    maskadd_h = nc.dram_tensor("maskadd", [128, 192], BF16, kind="ExternalInput")
    ident_h = nc.dram_tensor("ident", [128, 128], BF16, kind="ExternalInput")
    cvals_h = nc.dram_tensor("cvals", [1, NCV], F32, kind="ExternalInput")
    vsbi_h = nc.dram_tensor("vsb_init", [128, NT * H * 65], BF16, kind="ExternalInput")
    out_h = nc.dram_tensor("out", [NTOK, D], F32, kind="ExternalOutput")

    ctx = ExitStack()
    sbt = lambda name, shape, dt: ctx.enter_context(nc.sbuf_tensor(name, shape, dt))
    pst = lambda name, shape, dt: ctx.enter_context(nc.psum_tensor(name, shape, dt))

    with ctx:
        x_sb = sbt("x_sb", [128, NT * D], F32)
        actT = sbt("actT", [128, 8 * NTOK], BF16)
        vsb = sbt("vsb", [128, NT * H * 65], BF16)
        qt = [sbt(f"qt{i}", [128, NTOK], BF16) for i in range(2)]
        QA = [sbt(f"QA{i}", [128, NTOK], BF16) for i in range(2)]
        KA = [sbt(f"KA{i}", [128, NTOK], BF16) for i in range(2)]
        wbuf = sbt("wbuf", [128, 8 * D], BF16)
        wstage = sbt("wstage", [128, D], F32)
        B0 = sbt("B0", [128, D], F32)
        B1 = sbt("B1", [128, D], F32)
        B5 = sbt("B5", [128, D], F32)
        Bi = sbt("Bi", [128, D], I32)
        sb1 = sbt("sb1", [128, D], BF16)
        sb2 = sbt("sb2", [128, D], BF16)
        sb3 = sbt("sb3", [128, D], BF16)
        sb4 = sbt("sb4", [128, D], BF16)
        xd = [sbt(f"xd{i}", [128, D], BF16) for i in range(2)]
        pt = [sbt(f"pt{i}", [128, 512], BF16) for i in range(2)]
        ytmp = sbt("ytmp", [128, 512], BF16)
        ytok = sbt("ytok", [128, 320], BF16)
        yn = sbt("yn", [128, 256], BF16)
        maskbin = sbt("maskbin_s", [128, 640], BF16)
        maskadd = sbt("maskadd_s", [128, 192], BF16)
        ident = sbt("ident_s", [128, 128], BF16)
        cpool = sbt("cpool", [128, NCV], F32)
        bf_row = sbt("bf_row", [1, D], BF16)
        boe_row = sbt("boe_row", [1, D], BF16)
        bq8 = sbt("bq8", [128, 8], F32)
        bk_pp = sbt("bk_pp", [128, 8], F32)
        bv_pp = sbt("bv_pp", [128, 8], BF16)
        bns = sbt("bns", [128, 12], F32)
        mv = sbt("mv", [128, 2], F32)
        stt_ = sbt("stt", [128, 8], F32)
        am = sbt("am", [128, 32], F32)
        scl = sbt("scl", [128, 32], F32)
        rcp = sbt("rcp", [128, 32], F32)

        psA = pst("psA", [128, 1024], F32)
        psB = pst("psB", [128, 1024], F32)
        psC = pst("psC", [128, 512], F32)
        psD = pst("psD", [128, 512], F32)
        psE = pst("psE", [128, 512], F32)
        psF = pst("psF", [128, 512], F32)

        P = Prog(nc, ctx)
        block = ctx.enter_context(nc.Block())

        for i in range(NT):
            P.emit("sync",
                   lambda s, i=i: s.dma_start(x_sb[:, D * i:D * (i + 1)],
                                              x_h[128 * i:128 * (i + 1), :]), 16)
        t_x = P.tok("sync")
        if "consts" not in DBG_SKIP:
            for dst, src in ((maskbin, maskbin_h),
                             (maskadd, maskadd_h), (ident, ident_h)):
                P.emit("sync", lambda s, d=dst, so=src: s.dma_start(d[:, :], so[:, :]), 16)
            P.emit("sync", lambda s: s.dma_start(
                cpool[:, :], cvals_h[:, :].broadcast_to([128, NCV])), 16)
        if "bias" not in DBG_SKIP:
            P.emit("sync", lambda s: s.dma_start(
                bq8[:, :], bq_h[0:1, :].rearrange("a (c p) -> (a p) c", p=128),
                allow_slow_non_contiguous=True), 16)
            P.emit("sync", lambda s: s.dma_start(
                bk_pp[:, :], bk_h[0:1, :].rearrange("a (c p) -> (a p) c", p=128),
                allow_slow_non_contiguous=True), 16)
            P.emit("sync", lambda s: s.dma_start(
                stt_[:, 0:8], bv_h[0:1, :].rearrange("a (c p) -> (a p) c", p=128),
                allow_slow_non_contiguous=True), 16)
            P.emit("sync", lambda s: s.dma_start(B1[0:1, 0:D], bf_h[:, :]), 16)
        t_cl = P.tok("sync")

        def f_init(v):
            _wait(v, t_cl)
            if "bias" not in DBG_SKIP:
                v.tensor_scalar(bq8[:, :], bq8[:, :], 0.125, None, op0=ALU.mult)
                v.tensor_copy(bv_pp[:, :], stt_[:, 0:8])
                v.tensor_copy(bf_row[:, :], B1[0:1, 0:D])
            return v.memset(ytmp[:, :], 0.0)
        t_init0 = P.emit("vector", f_init, 1)
        if "vsbimg" not in DBG_SKIP:
            t_vones = P.emit("sync", lambda s: s.dma_start(vsb[:, :], vsbi_h[:, :]), 16)

        tp_state = {"flip": 0, "prev": {}}

        def emit_transpose8(srcs, dest_ap, dep_tok, evac="vector"):
            prevE = tp_state["prev"].get("E")
            prevF = tp_state["prev"].get("F")

            def fmm(t, prevE=prevE, prevF=prevF, dep=dep_tok, srcs=tuple(srcs)):
                _wait(t, dep)
                _wait(t, prevE)
                _wait(t, prevF)
                for c in range(4):
                    P.MM(t, psE[:, 128 * c:128 * (c + 1)], srcs[c], ident[:, :],
                         True, True)
                    ins = P.MM(t, psF[:, 128 * c:128 * (c + 1)], srcs[4 + c],
                               ident[:, :], True, True)
                return ins
            t_mm = P.emit("tensor", fmm, 1)

            def fev(e, t_mm=t_mm, dest=dest_ap, evac=evac):
                _wait(e, t_mm)
                if isinstance(dest, tuple):
                    dE, dF = dest
                    sE = psE[:, 0:512].rearrange("p (c n) -> p c n", n=128)
                    sF = psF[:, 0:512].rearrange("p (c n) -> p c n", n=128)
                else:
                    dE, dF = dest[:, 0:512], dest[:, 512:1024]
                    sE, sF = psE[:, 0:512], psF[:, 0:512]
                if evac == "vector":
                    e.tensor_copy(dE, sE)
                    return e.tensor_copy(dF, sF)
                e.copy(dE, sE)
                return e.copy(dF, sF)
            tk = P.emit("vector" if evac == "vector" else "scalar", fev, 1)
            tp_state["prev"]["E"] = tk
            tp_state["prev"]["F"] = tk
            return tk

        def emit_quant(out_bf, rstd_col, dep_toks, out_dep):
            def f1(v, deps=tuple(dep_toks)):
                for t in deps:
                    _wait(v, t)
                x3 = B0[:, :].rearrange("p (b k) -> p b k", k=32)
                v.tensor_reduce(am[:, :], x3, axis=AX.X, op=ALU.max,
                                apply_absolute_value=True)
                v.tensor_scalar(scl[:, :], am[:, :], C7I, 1e-30, op0=ALU.mult, op1=ALU.max)
                v.tensor_scalar(rcp[:, :], am[:, :], C7I, 1e-12, op0=ALU.mult, op1=ALU.max)
                ins = v.reciprocal(rcp[:, :], rcp[:, :])
                if rstd_col is not None:
                    ins = v.tensor_scalar(scl[:, :], scl[:, :], rstd_col, None, op0=ALU.mult)
                return ins
            t_small = P.emit("vector", f1, 1)

            def f2(s, t_small=t_small):
                _wait(s, t_small)
                e1 = B1[:, :].rearrange("p (b k) -> p b k", k=32)
                e5 = B5[:, :].rearrange("p (b k) -> p b k", k=32)
                s.copy(e5, rcp[:, :].unsqueeze(2).broadcast_to([128, 32, 32]))
                return s.copy(e1, scl[:, :].unsqueeze(2).broadcast_to([128, 32, 32]))
            t_exp = P.emit("scalar", f2, 1)

            def f3(v, t_exp=t_exp):
                _wait(v, t_exp)
                return v.tensor_tensor(B0[:, :], B0[:, :], B5[:, :], op=ALU.mult)
            t_u = P.emit("vector", f3, 1)

            def f4(s, t_u=t_u):
                _wait(s, t_u)
                return s.activation(sb1[:, :], B0[:, :], AF.Square)
            t_sq = P.emit("scalar", f4, 1)

            def f5(v, t_sq=t_sq, out_dep=out_dep, out_bf=out_bf):
                _wait(v, t_sq)
                _wait(v, out_dep)
                v.tensor_scalar(sb2[:, :], sb1[:, :], 4.0, -4.0, op0=ALU.is_ge, op1=ALU.mult)
                v.tensor_scalar(sb3[:, :], sb1[:, :], 16.0, -2.0, op0=ALU.is_ge, op1=ALU.mult)
                v.scalar_tensor_tensor(sb4[:, :], sb2[:, :], 8.0, sb3[:, :],
                                       op0=ALU.add, op1=ALU.add)
                v.scalar_tensor_tensor(B0[:, :], B0[:, :], -1.0, sb4[:, :],
                                       op0=ALU.mult, op1=ALU.mult)
                v.tensor_scalar(Bi[:, :], B0[:, :], 64.5, None, op0=ALU.add)
                v.tensor_scalar(sb2[:, :], sb1[:, :], 4.0, 0.125, op0=ALU.is_ge, op1=ALU.mult)
                v.tensor_scalar(sb3[:, :], sb1[:, :], 16.0, 0.25, op0=ALU.is_ge, op1=ALU.mult)
                v.scalar_tensor_tensor(sb4[:, :], sb2[:, :], 0.125, sb3[:, :],
                                       op0=ALU.add, op1=ALU.add)
                v.tensor_copy(B5[:, :], Bi[:, :])
                v.scalar_tensor_tensor(B0[:, :], B5[:, :], -64.5, B0[:, :],
                                       op0=ALU.add, op1=ALU.is_gt)
                v.scalar_tensor_tensor(B5[:, :], B5[:, :], -1.0, B0[:, :],
                                       op0=ALU.mult, op1=ALU.add)
                v.scalar_tensor_tensor(B0[:, :], B5[:, :], 64.0, sb4[:, :],
                                       op0=ALU.add, op1=ALU.mult)
                return v.tensor_tensor(out_bf[:, :], B0[:, :], B1[:, :], op=ALU.mult)
            return P.emit("vector", f5, 1)

        def wd_stage(r):
            buf = (QA[0], QA[1], KA[0], KA[1])[r // 2]
            return buf[:, D * (r % 2):D * (r % 2 + 1)]

        def emit_wd(qa_dep):
            tq = None
            for r in range(8):
                prev_v = P.tok("vector")

                def fl(s, r=r, prev_v=prev_v):
                    _wait(s, prev_v)
                    return s.dma_start(wstage[:, :], wf_h[128 * r:128 * (r + 1), :])
                t_ld = P.emit("sync", fl, 16)

                def fc(v, t_ld=t_ld, qa_dep=qa_dep):
                    _wait(v, t_ld)
                    _wait(v, qa_dep)
                    return v.tensor_copy(B0[:, :], wstage[:, :])
                P.emit("vector", fc, 1)
                tq = emit_quant(wd_stage(r), None, [], None)
            last = None
            for kc in range(8):
                srcs = [wd_stage(r)[:, 128 * kc:128 * (kc + 1)] for r in range(8)]
                last = emit_transpose8(srcs, wbuf[:, D * kc:D * (kc + 1)], tq,
                                       evac="scalar")
            return last

        def emit_ffn(t_wd_ready):
            resid_tok = [None, None]
            xd_free = [None, None]
            pend = None

            def flush_resid(pend):
                pi, pbank, pt_mm = pend

                def fres(v, pi=pi, pbank=pbank, pt_mm=pt_mm):
                    _wait(v, pt_mm)
                    return v.scalar_tensor_tensor(x_sb[:, D * pi:D * (pi + 1)],
                                                  pbank[:, :], 0.5,
                                                  x_sb[:, D * pi:D * (pi + 1)],
                                                  op0=ALU.mult, op1=ALU.add)
                resid_tok[pi % 2] = P.emit("vector", fres, 1)

            for i in range(NT):
                xt = x_sb[:, D * i:D * (i + 1)]

                def fbn(v, xt=xt, i=i):
                    if i == 0:
                        _wait(v, t_x)
                    v.bn_stats(bns[:, 0:6], xt[:, 0:512])
                    v.bn_stats(bns[:, 6:12], xt[:, 512:1024])
                    return v.bn_aggr(mv[:, :], bns[:, :])
                t_bn = P.emit("vector", fbn, 1)

                def fsq(s, t_bn=t_bn):
                    _wait(s, t_bn)
                    return s.activation(stt_[:, 0:1], mv[:, 1:2], AF.Sqrt,
                                        bias=cpool[:, 0:1])
                t_sd = P.emit("scalar", fsq, 1)

                def frc(v, t_sd=t_sd):
                    _wait(v, t_sd)
                    v.reciprocal(stt_[:, 1:2], stt_[:, 0:1])
                    return v.tensor_scalar(stt_[:, 2:3], mv[:, 0:1], -1.0, None,
                                           op0=ALU.mult)
                t_st = P.emit("vector", frc, 1)

                def fxm(s, xt=xt, t_st=t_st):
                    _wait(s, t_st)
                    return s.activation(B0[:, :], xt, AF.Identity, bias=stt_[:, 2:3])
                t_xm = P.emit("scalar", fxm, 1)

                xb = xd[i % 2]
                tq = emit_quant(xb, stt_[:, 1:2], [t_xm], xd_free[i % 2])
                if pend is not None:
                    flush_resid(pend)
                srcs = [xb[:, 128 * c:128 * (c + 1)] for c in range(8)]
                av = actT[:, :].rearrange("p (c n) -> p c n", n=2048)
                t_tr = emit_transpose8(
                    srcs, (av[:, 0:4, 128 * i:128 * (i + 1)],
                           av[:, 4:8, 128 * i:128 * (i + 1)]), tq,
                    evac="vector")
                xd_free[i % 2] = t_tr
                bankF = psA if i % 2 == 0 else psB
                prev_res = resid_tok[i % 2]

                def fm(t, i=i, bankF=bankF, t_tr=t_tr, prev_res=prev_res,
                       t_wd_ready=t_wd_ready):
                    _wait(t, t_tr)
                    _wait(t, t_wd_ready)
                    _wait(t, prev_res)
                    for kc in range(8):
                        for nb in range(2):
                            P.MM(t, bankF[:, 512 * nb:512 * (nb + 1)],
                                 actT[:, 2048 * kc + 128 * i:2048 * kc + 128 * (i + 1)],
                                 wbuf[:, D * kc + 512 * nb:D * kc + 512 * (nb + 1)],
                                 (kc == 0), False)
                    ins = None
                    for nb in range(2):
                        ins = P.MM(t, bankF[:, 512 * nb:512 * (nb + 1)],
                                   maskbin[0:1, 512:640],
                                   bf_row[:, 512 * nb:512 * (nb + 1)], False, True)
                    return ins
                t_mm = P.emit("tensor", fm, 1)
                pend = (i, bankF, t_mm)
            flush_resid(pend)
            return P.tok("vector")

        if PHASES >= 0.5:
            t_wd1 = emit_wd(None)
        else:
            t_wd1 = None
        def f_qainit(v, dep=t_wd1):
            _wait(v, dep)
            for b in range(2):
                v.memset(QA[b][64:128, :], 0.0)
                v.memset(QA[b][64:66, :], 1.0)
            v.memset(KA[0][64:128, :], 0.0)
            return v.memset(KA[1][64:128, :], 0.0)
        t_qainit = P.emit("vector", f_qainit, 1)

        if PHASES >= 1:
            t_ffn1 = emit_ffn(t_wd1)
        else:
            def fbar(v):
                _wait(v, t_x)
                _wait(v, t_wd1)
                return v.nop()
            t_ffn1 = P.emit("vector", fbar, 1)

        def emit_attn():
            xd_free = [None, None]
            for i in range(NT):
                xt = x_sb[:, D * i:D * (i + 1)]

                def fbn(v, xt=xt):
                    v.bn_stats(bns[:, 0:6], xt[:, 0:512])
                    v.bn_stats(bns[:, 6:12], xt[:, 512:1024])
                    return v.bn_aggr(mv[:, :], bns[:, :])
                t_bn = P.emit("vector", fbn, 1)

                def fsq(s, t_bn=t_bn):
                    _wait(s, t_bn)
                    return s.activation(stt_[:, 0:1], mv[:, 1:2], AF.Sqrt,
                                        bias=cpool[:, 0:1])
                t_sd = P.emit("scalar", fsq, 1)

                def frc(v, t_sd=t_sd):
                    _wait(v, t_sd)
                    v.reciprocal(stt_[:, 1:2], stt_[:, 0:1])
                    v.tensor_tensor(stt_[:, 2:3], mv[:, 0:1], stt_[:, 1:2], op=ALU.mult)
                    return v.tensor_scalar(stt_[:, 2:3], stt_[:, 2:3], -1.0, None,
                                           op0=ALU.mult)
                t_st = P.emit("vector", frc, 1)

                lb = xd[i % 2]

                def fln(s, xt=xt, lb=lb, t_st=t_st, free=xd_free[i % 2]):
                    _wait(s, t_st)
                    _wait(s, free)
                    return s.activation(lb[:, :], xt, AF.Identity,
                                        bias=stt_[:, 2:3], scale=stt_[:, 1:2])
                t_ln = P.emit("scalar", fln, 1)
                srcs = [lb[:, 128 * c:128 * (c + 1)] for c in range(8)]
                av = actT[:, :].rearrange("p (c n) -> p c n", n=2048)
                xd_free[i % 2] = emit_transpose8(
                    srcs, (av[:, 0:4, 128 * i:128 * (i + 1)],
                           av[:, 4:8, 128 * i:128 * (i + 1)]), t_ln,
                    evac="vector")
            t_lnT = xd_free[(NT - 1) % 2]

            for kc in range(8):
                prev_v = P.tok("vector")

                def fl(s, kc=kc, prev_v=prev_v):
                    _wait(s, prev_v)
                    return s.dma_start(wstage[:, :], wvt_h[128 * kc:128 * (kc + 1), :])
                t_ld = P.emit("sync", fl, 16)

                def fc(v, kc=kc, t_ld=t_ld):
                    _wait(v, t_ld)
                    return v.tensor_copy(wbuf[:, D * kc:D * (kc + 1)], wstage[:, :])
                P.emit("vector", fc, 1)
            t_wv = P.tok("vector")
            v_ev = [None, None]
            for i in range(NT):
                for nb in range(2):
                    half = nb
                    def fmv(t, i=i, half=half, nb=nb, prev=v_ev[half], t_wv=t_wv):
                        _wait(t, t_wv)
                        _wait(t, t_lnT)
                        _wait(t, prev)
                        ins = None
                        for kc in range(8):
                            ins = t.matmul(psA[:, 512 * half:512 * (half + 1)],
                                     actT[:, 2048 * kc + 128 * i:2048 * kc + 128 * (i + 1)],
                                     wbuf[:, D * kc + 512 * nb:D * kc + 512 * (nb + 1)],
                                     start=(kc == 0), stop=(kc == 7))
                        return ins
                    t_mmv = P.emit("tensor", fmv, 1)

                    def fev(v, i=i, half=half, nb=nb, t_mmv=t_mmv):
                        _wait(v, t_mmv)
                        va = vsb[:, :].rearrange("p (t h c) -> p t h c", t=NT, c=65)
                        src_ = psA[:, 512 * half:512 * (half + 1)].rearrange(
                            "p (h c) -> p h c", c=64)
                        return v.tensor_copy(va[:, i, 8 * nb:8 * (nb + 1), 0:64], src_)
                    v_ev[half] = P.emit("vector", fev, 1)
            t_vp = P.tok("vector")

            blocks = {}
            for h in range(H):
                s_h = float(SLOPES[h])
                nq = NQ[h]
                bl = []
                for q0 in range(0, NTOK, nq):
                    qlast = q0 + nq - 1
                    t_hi = qlast // 128
                    t_lo = max(0, int(math.ceil((q0 - THRESH / s_h - 127.0) / 128.0)))
                    bl.append((q0, t_lo, t_hi))
                blocks[h] = bl

            qk_prev = [v_ev[0], v_ev[1]]
            t_rehoused = [None, None]
            t_attn_done = [None, None]
            wqk_cast = [None, None]
            st_prev = [None, None]
            pt_prev = [None, None]
            pend_pv = [None]
            y_ev_bank = [None, None]
            tp_y = tp_state

            for m in range(8):
                par = m % 2
                prev_cast = wqk_cast[par]

                def flq(s, m=m, prev_cast=prev_cast):
                    _wait(s, prev_cast)
                    if m == 0:
                        _wait(s, t_vp)
                    return s.dma_start(wstage[:, :], wqt_h[m, :, :])
                t_lq = P.emit("sync", flq, 16)

                def fcq(v, par=par, t_lq=t_lq):
                    _wait(v, t_lq)
                    return v.tensor_copy(wbuf[:, 1024 * par:1024 * (par + 1)], wstage[:, :])
                t_cq = P.emit("vector", fcq, 1)

                def flk(s, m=m, t_cq=t_cq):
                    _wait(s, t_cq)
                    return s.dma_start(wstage[:, :], wkt_h[m, :, :])
                t_lk = P.emit("sync", flk, 16)

                def fck(v, par=par, t_lk=t_lk):
                    _wait(v, t_lk)
                    return v.tensor_copy(wbuf[:, 2048 + 1024 * par:2048 + 1024 * (par + 1)],
                                         wstage[:, :])
                wqk_cast[par] = P.emit("vector", fck, 1)
                t_ck = wqk_cast[par]

                reh_dep = t_rehoused[par]
                t_rh_parts = []
                for which, wofs, bias_ap, scale in (
                        ("q", 1024 * par, bq8, 0.125),
                        ("k", 2048 + 1024 * par, bk_pp, 1.0)):
                    if which == "k":
                        reh_dep = t_rh_parts[0]
                    for g4 in range(4):
                        half = g4 % 2

                        def fmq(t, wofs=wofs, g4=g4, half=half, t_ck=t_ck,
                                prev=qk_prev[half], reh_dep=reh_dep):
                            _wait(t, t_ck)
                            _wait(t, t_lnT)
                            _wait(t, prev)
                            _wait(t, reh_dep)
                            ins = None
                            for kc in range(8):
                                ins = t.matmul(psA[:, 512 * half:512 * (half + 1)],
                                         wbuf[:, wofs + 128 * kc:wofs + 128 * (kc + 1)],
                                         actT[:, 2048 * kc + 512 * g4:2048 * kc + 512 * (g4 + 1)],
                                         start=(kc == 0), stop=(kc == 7),
                                         skip_group_check=True)
                            return ins
                        t_mq = P.emit("tensor", fmq, 1)

                        def fevq(s, bias_ap=bias_ap, scale=scale, par=par,
                                 g4=g4, half=half, m=m, t_mq=t_mq, reh_dep=reh_dep):
                            _wait(s, t_mq)
                            _wait(s, reh_dep)
                            return s.activation(qt[par][:, 512 * g4:512 * (g4 + 1)],
                                                psA[:, 512 * half:512 * (half + 1)],
                                                AF.Identity, bias=bias_ap[:, m:m + 1],
                                                scale=scale)
                        qk_prev[half] = P.emit("scalar", fevq, 1)
                    t_qk = P.tok("scalar")

                    def frh_v(v, which=which, par=par, t_qk=t_qk,
                              prev_att0=t_attn_done[0], prev_att1=t_attn_done[1]):
                        _wait(v, t_qk)
                        _wait(v, prev_att0)
                        _wait(v, prev_att1)
                        if which == "q":
                            _wait(v, t_qainit)
                        ins = None
                        for hh in range(2):
                            buf = QA[hh] if which == "q" else KA[hh]
                            ins = v.tensor_copy(buf[0:64, :],
                                                qt[par][64 * hh:64 * (hh + 1), :])
                        return ins
                    t_rhv = P.emit("vector", frh_v, 1)
                    if which == "k":
                        def frh_g(g, m=m, t_rhv=t_rhv):
                            _wait(g, t_rhv)
                            for hh in range(2):
                                h = 2 * m + hh
                                g.dma_start(KA[hh][64:65, :],
                                            alibi_h[h:h + 1, :]).then_inc(P.ctr["sync"].sem, 16)
                                ins = g.dma_start(KA[hh][65:66, :],
                                                  alibi_h[16 + h:17 + h, :])
                                if hh == 0:
                                    ins.then_inc(P.ctr["sync"].sem, 16)
                            return ins
                        P.ctr["sync"].n += 16 * 3
                        t_rh_parts.append(P.emit("sync", frh_g, 16))
                    else:
                        t_rh_parts.append(t_rhv)
                t_rehoused[par] = t_rh_parts[1]
                t_rh = t_rehoused[par]

                for hh in range(2):
                    h = 2 * m + hh
                    s_h = float(SLOPES[h])
                    nq = NQ[h]
                    gsize = max(1, 512 // nq)

                    def flush_pv():
                        if pend_pv[0] is not None:
                            fn, pb_ = pend_pv[0]
                            pt_prev[pb_] = P.emit("tensor", fn, 1)
                            pend_pv[0] = None
                    for qg in range(NTOK // 512):
                        yb = qg % 2
                        ybank = psC if yb == 0 else psD
                        q0s = [b for b in blocks[h] if 512 * qg <= b[0] < 512 * (qg + 1)]
                        first_of_bank = True
                        prev_y_ev = y_ev_bank[yb]
                        for (q0, t_lo, t_hi) in q0s:
                            qlast = q0 + nq - 1
                            cb_ap = cpool[:, CIDX[(h, q0)]:CIDX[(h, q0)] + 1]
                            ts_list = list(range(t_lo, t_hi + 1))
                            total_pv = len(ts_list)
                            pv_done = 0
                            for gi in range(0, len(ts_list), gsize):
                                grp = ts_list[gi:gi + gsize]
                                sh = st_flip[0]
                                st_flip[0] ^= 1

                                def fst(t, grp=grp, sh=sh, q0=q0, nq=nq, hh=hh,
                                        prev_exp=st_prev[sh], t_rh=t_rh):
                                    _wait(t, t_rh)
                                    _wait(t, prev_exp)
                                    ins = None
                                    for j, tt in enumerate(grp):
                                        ins = t.matmul(
                                            psB[:, 512 * sh + nq * j:512 * sh + nq * (j + 1)],
                                            KA[hh][:, 128 * tt:128 * (tt + 1)],
                                            QA[hh][:, q0:q0 + nq],
                                            start=(j == 0), stop=(j == len(grp) - 1))
                                    return ins
                                t_stm = P.emit("tensor", fst, 1)
                                flush_pv()

                                t_pm = t_stm
                                diag = [(j, tt) for j, tt in enumerate(grp)
                                        if 128 * tt + 127 > q0]
                                if h == 0 and diag:
                                    def fpm(v, diag=diag, sh=sh, q0=q0, nq=nq,
                                            t_stm=t_stm):
                                        _wait(v, t_stm)
                                        for j, tt in diag:
                                            w = min(q0 + nq - 1, 128 * tt + 126) - q0 + 1
                                            jo = q0 - 128 * tt + 32
                                            ins = v.tensor_tensor(
                                                psB[:, 512 * sh + nq * j:512 * sh + nq * j + w],
                                                psB[:, 512 * sh + nq * j:512 * sh + nq * j + w],
                                                maskadd[:, jo:jo + w], op=ALU.add)
                                        return ins
                                    t_pm = P.emit("vector", fpm, 1)

                                pb = sh
                                gw = nq * len(grp)

                                def fex(s, sh=sh, pb=pb, gw=gw, cb_ap=cb_ap,
                                        t_pm=t_pm, prev_pv=pt_prev[pb]):
                                    _wait(s, t_pm)
                                    _wait(s, prev_pv)
                                    return s.activation(pt[pb][:, 0:gw],
                                                        psB[:, 512 * sh:512 * sh + gw],
                                                        AF.Exp, bias=cb_ap)
                                t_exp = P.emit("scalar", fex, 1)
                                st_prev[sh] = t_exp

                                t_msk = t_exp
                                if h != 0 and diag:
                                    def fpo(v, diag=diag, pb=pb, q0=q0, nq=nq,
                                            t_exp=t_exp):
                                        _wait(v, t_exp)
                                        for j, tt in diag:
                                            w = min(q0 + nq - 1, 128 * tt + 126) - q0 + 1
                                            jo = q0 - 128 * tt + 512
                                            ins = v.tensor_tensor(
                                                pt[pb][:, nq * j:nq * j + w],
                                                pt[pb][:, nq * j:nq * j + w],
                                                maskbin[:, jo:jo + w], op=ALU.mult)
                                        return ins
                                    t_msk = P.emit("vector", fpo, 1)

                                def fpv(t, grp=grp, pb=pb, q0=q0, nq=nq, qg=qg,
                                        ybank=ybank, t_msk=t_msk, h=h,
                                        first=first_of_bank, pv0=pv_done,
                                        total=total_pv, prev_y_ev=prev_y_ev):
                                    _wait(t, t_msk)
                                    if first:
                                        _wait(t, prev_y_ev)
                                    ins = None
                                    for j, tt in enumerate(grp):
                                        ins = t.matmul(
                                            ybank[0:65, q0 - 512 * qg:q0 - 512 * qg + nq],
                                            vsb[:, (H * 65) * tt + 65 * h:(H * 65) * tt + 65 * (h + 1)],
                                            pt[pb][:, nq * j:nq * (j + 1)],
                                            start=(pv0 + j == 0),
                                            stop=(pv0 + j == total - 1),
                                            skip_group_check=True)
                                    return ins
                                pend_pv[0] = (fpv, pb)
                                first_of_bank = False
                                pv_done += len(grp)
                        flush_pv()
                        t_pv_last = P.tok("tensor")

                        def fye(s, ybank=ybank, t_pv_last=t_pv_last):
                            _wait(s, t_pv_last)
                            return s.activation(ytmp[0:65, :], ybank[0:65, :], AF.Identity)
                        t_ye = P.emit("scalar", fye, 1)

                        prevE = tp_state["prev"].get("E")
                        prevF = tp_state["prev"].get("F")

                        def fyt(t, t_ye=t_ye, prevE=prevE, prevF=prevF):
                            _wait(t, t_ye)
                            _wait(t, prevE)
                            _wait(t, prevF)
                            P.MM(t, psE[:, 0:80], ytmp[0:80, 0:128],
                                 ident[0:80, 0:80], True, True)
                            P.MM(t, psF[:, 0:80], ytmp[0:80, 256:384],
                                 ident[0:80, 0:80], True, True)
                            P.MM(t, psE[:, 80:160], ytmp[0:80, 128:256],
                                 ident[0:80, 0:80], True, True)
                            return P.MM(t, psF[:, 80:160], ytmp[0:80, 384:512],
                                        ident[0:80, 0:80], True, True)
                        t_yt = P.emit("tensor", fyt, 1)

                        def fyv(v, t_yt=t_yt):
                            _wait(v, t_yt)
                            v.tensor_copy(ytok[:, 0:160], psE[:, 0:160])
                            v.tensor_copy(ytok[:, 160:320], psF[:, 0:160])
                            sc_ap = ytok[:, :].rearrange("p (c k) -> p c k", k=80)[:, :, 64:65].squeeze(2)
                            v.tensor_scalar(stt_[:, 4:8], sc_ap, 8192.0, None, op0=ALU.mult)
                            v.reciprocal(stt_[:, 4:8], stt_[:, 4:8])
                            v.tensor_scalar(stt_[:, 4:8], stt_[:, 4:8], 8192.0, None, op0=ALU.mult)
                            ins = None
                            for c in range(4):
                                cc = c
                                ins = v.tensor_scalar(yn[:, 64 * cc:64 * (cc + 1)],
                                                      ytok[:, 80 * c:80 * c + 64],
                                                      stt_[:, 4 + c:5 + c], None,
                                                      op0=ALU.mult)
                            return ins
                        t_yn = P.emit("vector", fyv, 1)
                        tp_state["prev"]["E"] = t_yn
                        tp_state["prev"]["F"] = t_yn

                        def fyb(t, t_yn=t_yn):
                            _wait(t, t_yn)
                            P.MM(t, psE[0:64, 0:128], yn[:, 0:64], ident[:, :],
                                 True, True)
                            P.MM(t, psF[0:64, 0:128], yn[:, 128:192], ident[:, :],
                                 True, True)
                            P.MM(t, psE[0:64, 128:256], yn[:, 64:128], ident[:, :],
                                 True, True)
                            return P.MM(t, psF[0:64, 128:256], yn[:, 192:256],
                                        ident[:, :], True, True)
                        t_yb = P.emit("tensor", fyb, 1)

                        def fyw2(v, h=h, qg=qg, t_yb=t_yb):
                            _wait(v, t_yb)
                            full = yt_ap_full(h // 2, 512 * qg, 512)
                            r0 = 64 * (h % 2)
                            v.tensor_copy(full[r0:r0 + 64, 0:256], psE[0:64, 0:256])
                            return v.tensor_copy(full[r0:r0 + 64, 256:512],
                                                 psF[0:64, 0:256])
                        t_fyw = P.emit("vector", fyw2, 1)
                        tp_state["prev"]["E"] = t_fyw
                        tp_state["prev"]["F"] = t_fyw
                        y_ev_bank[yb] = t_fyw
                    t_attn_done[hh] = P.tok("tensor")
            t_attn_v = P.tok("vector")
            t_attn_s = P.tok("scalar")

            for kc in range(8):
                prev_v = P.tok("vector")

                def fl(s, kc=kc, prev_v=prev_v):
                    _wait(s, prev_v)
                    return s.dma_start(wstage[:, :], wot_h[128 * kc:128 * (kc + 1), :])
                t_ld = P.emit("sync", fl, 16)

                def fc(v, kc=kc, t_ld=t_ld):
                    _wait(v, t_ld)
                    return v.tensor_copy(wbuf[:, D * kc:D * (kc + 1)], wstage[:, :])
                P.emit("vector", fc, 1)
            t_wo = P.tok("vector")

            def fbvw(t, t_wo=t_wo, t_attn_s=t_attn_s):
                _wait(t, t_wo)
                _wait(t, t_attn_s)
                for nb in range(2):
                    for kc in range(8):
                        ins = t.matmul(psB[0:1, 512 * nb:512 * (nb + 1)],
                                 bv_pp[:, kc:kc + 1],
                                 wbuf[:, D * kc + 512 * nb:D * kc + 512 * (nb + 1)],
                                 start=(kc == 0), stop=(kc == 7))
                return ins
            t_bvw = P.emit("tensor", fbvw, 1)

            def flbo(s, t_wo=t_wo):
                _wait(s, t_wo)
                return s.dma_start(wstage[0:1, 0:D], bo_h[:, :])
            t_bo2 = P.emit("sync", flbo, 16)

            def fboe(v, t_bvw=t_bvw, t_bo2=t_bo2):
                _wait(v, t_bvw)
                _wait(v, t_bo2)
                return v.tensor_tensor(boe_row[:, :], wstage[0:1, 0:D],
                                       psB[0:1, 0:D], op=ALU.add)
            t_boe = P.emit("vector", fboe, 1)

            mm_prev = [None, None]
            for i in range(NT):
                xt = x_sb[:, D * i:D * (i + 1)]
                bankF = psA if i % 2 == 0 else psB

                def fm(t, i=i, bankF=bankF, prev=mm_prev[i % 2], t_boe=t_boe,
                       t_attn_v=t_attn_v):
                    _wait(t, t_boe)
                    _wait(t, t_attn_v)
                    _wait(t, prev)
                    ins = None
                    for nb in range(2):
                        for kc in range(8):
                            t.matmul(bankF[:, 512 * nb:512 * (nb + 1)],
                                     yt_ap_full(kc, 128 * i, 128),
                                     wbuf[:, D * kc + 512 * nb:D * kc + 512 * (nb + 1)],
                                     start=(kc == 0), stop=False)
                        ins = t.matmul(bankF[:, 512 * nb:512 * (nb + 1)],
                                 maskbin[0:1, 512:640], boe_row[:, 512 * nb:512 * (nb + 1)],
                                 start=False, stop=True)
                    return ins
                t_mm = P.emit("tensor", fm, 1)

                def fres(v, xt=xt, bankF=bankF, t_mm=t_mm):
                    _wait(v, t_mm)
                    return v.scalar_tensor_tensor(xt, bankF[:, :], 1.0, xt,
                                                  op0=ALU.mult, op1=ALU.add)
                mm_prev[i % 2] = P.emit("vector", fres, 1)
            return P.tok("vector")

        def yt_ap_full(c, col0, width):
            homes = {
                0: bass.AP(B0, 0, [[D, 128], [1, D]]).bitcast(BF16),
                1: bass.AP(B1, 0, [[D, 128], [1, D]]).bitcast(BF16),
                2: bass.AP(B5, 0, [[D, 128], [1, D]]).bitcast(BF16),
                3: bass.AP(Bi, 0, [[D, 128], [1, D]]).bitcast(BF16),
                6: qt[0][:, :],
                7: qt[1][:, :],
            }
            if c in homes:
                return homes[c][:, col0:col0 + width]
            if c == 4:
                tns = sb1 if col0 < 1024 else sb2
            else:
                tns = sb3 if col0 < 1024 else sb4
            o = col0 % 1024
            return tns[:, o:o + width]

        st_flip = [0]

        if PHASES >= 2:
            t_attn = emit_attn()
        else:
            t_attn = t_ffn1

        if PHASES >= 3:
            t_wd2 = emit_wd(t_attn)
            t_ffn2 = emit_ffn(t_wd2)
            t_final = t_ffn2
        else:
            t_final = t_attn

        for i in range(NT):
            def fst_(s, i=i, t_final=t_final):
                if i == 0:
                    _wait(s, t_final)
                return s.dma_start(out_h[128 * i:128 * (i + 1), :],
                                   x_sb[:, D * i:D * (i + 1)])
            P.emit("sync", fst_, 16)
        t_done = P.tok("sync")

        def ffin(s):
            _wait(s, t_done)
            return None
        P.emit("sync", ffin, 0)

        @block.sync
        def _(sync):
            P.run("sync", sync)

        @block.scalar
        def _(scalar):
            P.run("scalar", scalar)

        @block.vector
        def _(vector):
            P.run("vector", vector)

        @block.tensor
        def _(tensor):
            P.run("tensor", tensor)

        @block.gpsimd
        def _(gpsimd):
            P.run("gpsimd", gpsimd)

    return nc


def _vsb_init():
    v = np.zeros((128, NT * H * 65), dtype=ml_dtypes.bfloat16)
    v3 = v.reshape(128, NT * H, 65)
    v3[:, :, 64] = 1.0
    return v


def _host_consts():
    k = np.arange(NTOK, dtype=np.float64)
    alibi = np.zeros((32, NTOK), dtype=ml_dtypes.bfloat16)
    for h in range(H):
        v = (np.float32(SLOPES[h]) * (k - 2047.0).astype(np.float32)).astype(np.float32)
        A = (np.round(v / 8.0) * 8.0).astype(np.float32)
        B = (v - A).astype(np.float32)
        alibi[h] = A.astype(ml_dtypes.bfloat16)
        alibi[16 + h] = B.astype(ml_dtypes.bfloat16)
    kk = np.arange(128)[:, None]
    jj = np.arange(640)[None, :]
    maskbin = (kk <= (jj - 512)).astype(ml_dtypes.bfloat16)
    jj3 = np.arange(480, 672)[None, :]
    maskadd = np.where(kk <= (jj3 - 512), np.float32(0.0),
                       np.float32(-1e30)).astype(ml_dtypes.bfloat16)
    ident = np.eye(128, dtype=ml_dtypes.bfloat16)
    return alibi, maskbin, maskadd, ident


def kernel(L0_attn_x, L0_attn_Wq, L0_attn_bq, L0_attn_Wk, L0_attn_bk,
           L0_attn_Wv, L0_attn_bv, L0_attn_Wo, L0_attn_bo,
           L0_ffn_weight, L0_ffn_bias):
    x = np.ascontiguousarray(np.asarray(L0_attn_x, dtype=np.float32))
    wq = np.asarray(L0_attn_Wq, np.float32)
    wk = np.asarray(L0_attn_Wk, np.float32)
    wv = np.asarray(L0_attn_Wv, np.float32)
    wo = np.asarray(L0_attn_Wo, np.float32)
    wf = np.ascontiguousarray(np.asarray(L0_ffn_weight, np.float32))
    wqt = np.ascontiguousarray(
        wq.T.reshape(8, 128, 8, 128).transpose(2, 1, 0, 3).reshape(8, 128, D))
    wkt = np.ascontiguousarray(
        wk.T.reshape(8, 128, 8, 128).transpose(2, 1, 0, 3).reshape(8, 128, D))
    wvt = np.ascontiguousarray(wv.T)
    wot = np.ascontiguousarray(wo.T)
    alibi, maskbin, maskadd, ident = _host_consts()

    common = {
        "wqt": wqt, "wkt": wkt, "wvt": wvt, "wot": wot, "wf": wf,
        "bq": np.asarray(L0_attn_bq, np.float32).reshape(1, D),
        "bk": np.asarray(L0_attn_bk, np.float32).reshape(1, D),
        "bv": np.asarray(L0_attn_bv, np.float32).reshape(1, D),
        "bo": np.asarray(L0_attn_bo, np.float32).reshape(1, D),
        "bf": np.asarray(L0_ffn_bias, np.float32).reshape(1, D),
        "alibi": alibi, "maskbin": maskbin, "maskadd": maskadd, "ident": ident,
        "cvals": np.asarray(CVALS, dtype=np.float32).reshape(1, NCV),
        "vsb_init": _vsb_init(),
    }
    in_maps = [dict(common, x=np.ascontiguousarray(x[c])) for c in range(NCORES)]

    nc = build_nc()
    res = run_bass_kernel_spmd(nc, in_maps, core_ids=list(range(NCORES)))
    return np.stack([np.asarray(r["out"], dtype=np.float32)
                     for r in res.results], axis=0)
